# version 01
# baseline (speedup 1.0000x reference)
import numpy as np
import concourse.bass as bass
import concourse.mybir as mybir
from concourse.bass_utils import run_bass_kernel_spmd
from contextlib import ExitStack

F32 = mybir.dt.float32
BF16 = mybir.dt.bfloat16
ALU = mybir.AluOpType
AF = mybir.ActivationFunctionType
AX = mybir.AxisListType.X

TL = 1024
DM = 2048
NCH = DM // 128
FQKV = 6144
HFF = 5632
NEXP = 8
FEXP = 1408


class Ctr:
    def __init__(self, nc, name):
        self.sem = nc.semaphore(name).__enter__()
        self.n = 0

    def inc(self, k=1):
        self.n += k
        return self.n


def dma(eng, ctr, out, in_):
    eng.dma_start(out=out, in_=in_).then_inc(ctr.sem, 16)
    ctr.inc(16)


class SlotCtr:
    def __init__(self, nc, name, nslots):
        self.c = [Ctr(nc, f"{name}_{i}") for i in range(nslots)]
        self.nslots = nslots

    def dma(self, eng, slot, out, in_):
        dma(eng, self.c[slot % self.nslots], out, in_)

    def wait(self, eng, slot, k):
        if k > 0:
            eng.wait_ge(self.c[slot % self.nslots].sem, 16 * k)


def fm_hbm_ap(h):
    return h.ap().rearrange("(c p) t -> p c t", p=128)


def fm_sb_ap(sb, nch, tl=TL):
    return sb[:, 0:nch * tl].rearrange("p (c t) -> p c t", t=tl)


def host_constants():
    c = {}
    c["ident"] = np.eye(128, dtype=np.float32)
    c["onescol"] = np.ones((128, 1), np.float32)
    c["onesrow"] = np.ones((1, 128), np.float32)
    oh16 = np.zeros((16, 16 * 128), np.float32)
    for n in range(16):
        oh16[n, n * 128:(n + 1) * 128] = 1.0
    c["oh16"] = oh16
    oh8 = np.zeros((8, 8 * 128), np.float32)
    for n in range(8):
        oh8[n, n * 128:(n + 1) * 128] = 1.0
    c["oh8"] = oh8
    import ml_dtypes
    c["oh8_bf"] = oh8.astype(ml_dtypes.bfloat16)
    return c


def build_A(upto=4):
    nc = bass.Bass("TRN2", target_bir_lowering=False)
    xsl = nc.dram_tensor("xsl", [TL, DM], F32, kind="ExternalInput")
    wqkv = nc.dram_tensor("wqkv", [DM, FQKV], F32, kind="ExternalInput")
    w1sl = nc.dram_tensor("w1sl", [256, HFF], F32, kind="ExternalInput")
    w2sl = nc.dram_tensor("w2sl", [256, HFF], F32, kind="ExternalInput")
    w3sl = nc.dram_tensor("w3sl", [704, DM], F32, kind="ExternalInput")
    identD = nc.dram_tensor("ident", [128, 128], F32, kind="ExternalInput")
    onescolD = nc.dram_tensor("onescol", [128, 1], F32, kind="ExternalInput")
    onesrowD = nc.dram_tensor("onesrow", [1, 128], F32, kind="ExternalInput")

    x_fm_o = nc.dram_tensor("x_fm", [DM, TL], F32, kind="ExternalOutput")
    xn_bf_o = nc.dram_tensor("xn_bf", [DM, TL], BF16, kind="ExternalOutput")
    q_fm_o = nc.dram_tensor("q_fm", [DM, TL], F32, kind="ExternalOutput")
    k_fm_o = nc.dram_tensor("k_fm", [DM, TL], F32, kind="ExternalOutput")
    kv_fm_o = nc.dram_tensor("kv_fm", [DM, TL], F32, kind="ExternalOutput")
    wabs_o = nc.dram_tensor("wabs", [1, 4], F32, kind="ExternalOutput")
    dbg_o = nc.dram_tensor("dbg", [1, 4 * TL], F32, kind="ExternalOutput")

    with ExitStack() as es:
        E = es.enter_context
        xt = [E(nc.sbuf_tensor(f"xt{i}", [128, DM], F32)) for i in range(2)]
        big_a = E(nc.sbuf_tensor("big_a", [128, NCH * TL], F32))
        xn_bf = E(nc.sbuf_tensor("xn_bf_s", [128, NCH * TL], BF16))
        big_b = E(nc.sbuf_tensor("big_b", [128, 2 * 2816], F32))
        ident = E(nc.sbuf_tensor("ident_s", [128, 128], F32))
        onescol = E(nc.sbuf_tensor("onescol_s", [128, 1], F32))
        onesrow = E(nc.sbuf_tensor("onesrow_s", [1, 128], F32))
        stat = E(nc.sbuf_tensor("stat", [1, 4 * TL], F32))
        sq = [E(nc.sbuf_tensor(f"sq{i}", [128, TL], F32)) for i in range(2)]
        qst = E(nc.sbuf_tensor("qst", [128, 4 * TL], F32))
        kvst = E(nc.sbuf_tensor("kvst", [128, 4 * TL], F32))
        ph1 = E(nc.sbuf_tensor("ph1", [128, 2 * 512], F32))
        ph2 = E(nc.sbuf_tensor("ph2", [128, 2 * 512], F32))
        wqb = [E(nc.sbuf_tensor(f"wqb{i}", [128, 384], BF16)) for i in range(2)]
        abscol = E(nc.sbuf_tensor("abscol", [128, 16], F32))
        bank = [E(nc.psum_tensor(f"bank{i}", [128, 512], F32)) for i in range(8)]

        ld = Ctr(nc, "ld")
        ldxt = SlotCtr(nc, "ldxt", 2)

        tr1 = Ctr(nc, "tr1")
        ev1 = Ctr(nc, "ev1")
        with nc.Block() as b:
            @b.sync
            def _(sync):
                dma(sync, ld, ident[:], identD[:, :])
                dma(sync, ld, onescol[:], onescolD[:, :])
                dma(sync, ld, onesrow[:], onesrowD[:, :])
                for tb in range(8):
                    if tb >= 2:
                        sync.wait_ge(tr1.sem, (tb - 1) * 16)
                    ldxt.dma(sync, tb, xt[tb % 2][:], xsl[tb * 128:(tb + 1) * 128, :])

            @b.tensor
            def _(tensor):
                tensor.wait_ge(ld.sem, 48)
                for tb in range(8):
                    ldxt.wait(tensor, tb, tb // 2 + 1)
                    for db in range(16):
                        i = tb * 16 + db
                        if i >= 2:
                            tensor.wait_ge(ev1.sem, i - 1)
                        tensor.transpose(bank[i % 2][:, 0:128],
                                         xt[tb % 2][:, db * 128:(db + 1) * 128],
                                         ident[:]).then_inc(tr1.sem, 1)
                        tr1.inc()

            @b.scalar
            def _(scalar):
                for tb in range(8):
                    for db in range(16):
                        i = tb * 16 + db
                        scalar.wait_ge(tr1.sem, i + 1)
                        scalar.copy(big_a[:, db * TL + tb * 128: db * TL + (tb + 1) * 128],
                                    bank[i % 2][:, 0:128]).then_inc(ev1.sem, 1)
                        ev1.inc()

        if upto < 2:
            return nc
        so_x = Ctr(nc, "so_x")
        ac2 = Ctr(nc, "ac2")
        pe2 = Ctr(nc, "pe2")
        dv2 = Ctr(nc, "dv2")
        M, V, R, W = 0, TL, 2 * TL, 3 * TL
        with nc.Block() as b:
            @b.sync
            def _(sync):
                dma(sync, so_x, fm_hbm_ap(x_fm_o), fm_sb_ap(big_a, NCH))

            @b.scalar
            def _(scalar):
                for dc in range(NCH):
                    if dc >= 2:
                        scalar.wait_ge(pe2.sem, 4 * (dc - 1))
                    scalar.activation(sq[dc % 2][:], big_a[:, dc * TL:(dc + 1) * TL],
                                      AF.Square).then_inc(ac2.sem, 1)
                    ac2.inc()
                scalar.wait_ge(dv2.sem, 7)
                scalar.activation(stat[0:1, W:W + TL], stat[0:1, R:R + TL],
                                  AF.Sqrt).then_inc(ac2.sem, 1)
                ac2.inc()

            @b.tensor
            def _(tensor):
                for dc in range(NCH):
                    tensor.wait_ge(ac2.sem, dc + 1)
                    st_, sp_ = (dc == 0), (dc == NCH - 1)
                    for tc in range(2):
                        tensor.matmul(bank[0 + tc][0:1, :], onescol[:],
                                      big_a[:, dc * TL + tc * 512: dc * TL + (tc + 1) * 512],
                                      start=st_, stop=sp_).then_inc(pe2.sem, 1)
                        pe2.inc()
                        tensor.matmul(bank[6 + tc][0:1, :], onescol[:],
                                      sq[dc % 2][:, tc * 512:(tc + 1) * 512],
                                      start=st_, stop=sp_).then_inc(pe2.sem, 1)
                        pe2.inc()
                tensor.wait_ge(dv2.sem, 8)
                for tc in range(2):
                    tensor.matmul(bank[2 + tc][:, :], onesrow[:],
                                  stat[0:1, M + tc * 512: M + (tc + 1) * 512],
                                  start=True, stop=True).then_inc(pe2.sem, 1)
                    pe2.inc()
                    tensor.matmul(bank[4 + tc][:, :], onesrow[:],
                                  stat[0:1, V + tc * 512: V + (tc + 1) * 512],
                                  start=True, stop=True).then_inc(pe2.sem, 1)
                    pe2.inc()

            @b.vector
            def _(vector):
                vector.wait_ge(pe2.sem, 64)
                vector.tensor_scalar_mul(stat[0:1, M:M + 512], bank[0][0:1, :],
                                         1.0 / DM).then_inc(dv2.sem, 1)
                vector.tensor_scalar_mul(stat[0:1, M + 512:M + TL], bank[1][0:1, :],
                                         1.0 / DM).then_inc(dv2.sem, 1)
                vector.tensor_scalar_mul(stat[0:1, V:V + 512], bank[6][0:1, :],
                                         1.0 / DM).then_inc(dv2.sem, 1)
                vector.tensor_scalar_mul(stat[0:1, V + 512:V + TL], bank[7][0:1, :],
                                         1.0 / DM).then_inc(dv2.sem, 1)
                vector.tensor_mul(stat[0:1, R:R + TL], stat[0:1, M:M + TL],
                                  stat[0:1, M:M + TL]).then_inc(dv2.sem, 1)
                vector.tensor_sub(stat[0:1, W:W + TL], stat[0:1, V:V + TL],
                                  stat[0:1, R:R + TL]).then_inc(dv2.sem, 1)
                vector.tensor_scalar_add(stat[0:1, R:R + TL], stat[0:1, W:W + TL],
                                         1e-5).then_inc(dv2.sem, 1)
                dv2.inc(7)
                vector.wait_ge(ac2.sem, 17)
                vector.reciprocal(stat[0:1, V:V + TL],
                                  stat[0:1, W:W + TL]).then_inc(dv2.sem, 1)
                dv2.inc()
                vector.wait_ge(pe2.sem, 68)
                for dc in range(NCH):
                    for tc in range(2):
                        sl = slice(dc * TL + tc * 512, dc * TL + (tc + 1) * 512)
                        vector.tensor_tensor(ph1[:, 0:512], big_a[:, sl],
                                             bank[2 + tc][:, :], ALU.subtract)
                        vector.tensor_tensor(xn_bf[:, sl], ph1[:, 0:512],
                                             bank[4 + tc][:, :], ALU.mult)

        if upto < 3:
            with nc.Block() as b:
                @b.sync
                def _(sync):
                    dma(sync, so_x, dbg_o[:, :], stat[0:1, :])
                    dma(sync, so_x, fm_hbm_ap(xn_bf_o), fm_sb_ap(xn_bf, NCH))
                    sync.wait_ge(so_x.sem, so_x.n)
            return nc
        wd3 = SlotCtr(nc, "wd3", 2)
        pc3 = Ctr(nc, "pc3")
        ev3 = Ctr(nc, "ev3")
        dv3 = Ctr(nc, "dv3")
        ac3 = Ctr(nc, "ac3")
        so_q = SlotCtr(nc, "so_q", 4)
        so_k = Ctr(nc, "so_k")
        so_kv = SlotCtr(nc, "so_kv", 4)
        so_xn = Ctr(nc, "so_xn")

        def piece_list():
            out = []
            for fg in range(16):
                for f3 in range(3):
                    for tc in range(2):
                        out.append((fg * 6 + f3 * 2 + tc, fg, fg * 3 + f3, tc))
            return out

        with nc.Block() as b:
            @b.gpsimd
            def _(gpsimd):
                for fg in range(16):
                    for kc in range(16):
                        i = fg * 16 + kc
                        if i >= 2:
                            gpsimd.wait_ge(pc3.sem, i - 1)
                        wd3.dma(gpsimd, i, wqb[i % 2][:],
                                wqkv[kc * 128:(kc + 1) * 128, fg * 384:(fg + 1) * 384])

            @b.tensor
            def _(tensor):
                for fg in range(16):
                    tensor.wait_ge(ev3.sem, 6 * fg)
                    for kc in range(16):
                        i = fg * 16 + kc
                        wd3.wait(tensor, i, i // 2 + 1)
                        nmm = 0
                        for f3 in range(3):
                            for tc in range(2):
                                bid = f3 * 2 + tc
                                mm = tensor.matmul(
                                    bank[bid][:, :],
                                    wqb[i % 2][:, f3 * 128:(f3 + 1) * 128],
                                    xn_bf[:, kc * TL + tc * 512: kc * TL + (tc + 1) * 512],
                                    start=(kc == 0), stop=(kc == 15))
                                nmm += 1
                        mm.then_inc(pc3.sem, 1)
                        pc3.inc()

            @b.vector
            def _(vector):
                did_sox_wait = [False]
                for (j, fg, fi, tc) in piece_list():
                    if j % 6 == 0:
                        vector.wait_ge(pc3.sem, 16 * (fg + 1))
                    bid = j % 6
                    sl2 = slice((j % 2) * 512, (j % 2) * 512 + 512)
                    if fi < 32:
                        vector.tensor_scalar_min(ph1[:, sl2], bank[bid][:, :],
                                                 0.0).then_inc(dv3.sem, 1)
                        dv3.inc()
                        if fi < 16:
                            if fi >= 4 and tc == 0:
                                so_q.wait(vector, fi, fi // 4)
                            dest = qst[:, (fi % 4) * TL + tc * 512:(fi % 4) * TL + (tc + 1) * 512]
                        else:
                            if not did_sox_wait[0]:
                                vector.wait_ge(so_x.sem, 16)
                                did_sox_wait[0] = True
                            dest = big_a[:, (fi - 16) * TL + tc * 512:(fi - 16) * TL + (tc + 1) * 512]
                        vector.wait_ge(ac3.sem, dv3.n)
                        vector.scalar_tensor_tensor(dest, bank[bid][:, :], 0.0,
                                                    ph2[:, sl2], ALU.max,
                                                    ALU.add).then_inc(ev3.sem, 1)
                        ev3.inc()
                    else:
                        jj = fi - 32
                        if jj >= 4 and tc == 0:
                            so_kv.wait(vector, jj, jj // 4)
                        vector.tensor_tensor(
                            kvst[:, (jj % 4) * TL + tc * 512:(jj % 4) * TL + (tc + 1) * 512],
                            bank[bid][:, :],
                            big_a[:, jj * TL + tc * 512: jj * TL + (tc + 1) * 512],
                            ALU.mult).then_inc(ev3.sem, 1)
                        ev3.inc()

            @b.scalar
            def _(scalar):
                nqk = 0
                for (j, fg, fi, tc) in piece_list():
                    if fi < 32:
                        nqk += 1
                        if j >= 2:
                            scalar.wait_ge(ev3.sem, j - 1)
                        scalar.wait_ge(dv3.sem, nqk)
                        scalar.activation(ph2[:, slice((j % 2) * 512, (j % 2) * 512 + 512)],
                                          ph1[:, slice((j % 2) * 512, (j % 2) * 512 + 512)],
                                          AF.Exp).then_inc(ac3.sem, 1)
                        ac3.inc()

            @b.sync
            def _(sync):
                dma(sync, so_xn, fm_hbm_ap(xn_bf_o), fm_sb_ap(xn_bf, NCH))
                for fi in range(48):
                    fg, f3 = divmod(fi, 3)
                    sync.wait_ge(ev3.sem, fg * 6 + f3 * 2 + 2)
                    if fi < 16:
                        so_q.dma(sync, fi, q_fm_o[fi * 128:(fi + 1) * 128, :],
                                 qst[:, (fi % 4) * TL:(fi % 4 + 1) * TL])
                    elif fi < 32:
                        jj = fi - 16
                        dma(sync, so_k, k_fm_o[jj * 128:(jj + 1) * 128, :],
                            big_a[:, jj * TL:(jj + 1) * TL])
                    else:
                        jj = fi - 32
                        so_kv.dma(sync, jj, kv_fm_o[jj * 128:(jj + 1) * 128, :],
                                  kvst[:, (jj % 4) * TL:(jj % 4 + 1) * TL])

        if upto < 4:
            with nc.Block() as b:
                @b.sync
                def _(sync):
                    sync.wait_ge(so_x.sem, so_x.n)
                    sync.wait_ge(so_xn.sem, so_xn.n)
                    for cc in so_q.c + so_kv.c:
                        sync.wait_ge(cc.sem, cc.n)
                    sync.wait_ge(so_k.sem, so_k.n)
            return nc
        ld4 = SlotCtr(nc, "ld4", 2)
        dv4 = Ctr(nc, "dv4")
        pe4 = Ctr(nc, "pe4")
        so_w = Ctr(nc, "so_w")
        slabs = []
        for wsl in (w1sl, w2sl):
            for r0 in (0, 128):
                for c0 in (0, 2816):
                    slabs.append((wsl, r0, r0 + 128, c0, 2816))
        for r0 in range(0, 704, 128):
            slabs.append((w3sl, r0, min(r0 + 128, 704), 0, DM))
        nsl = len(slabs)
        with nc.Block() as b:
            @b.vector
            def _(vector):
                vector.memset(abscol[:], 0.0)
                vector.memset(stat[0:1, 0:4], 0.0).then_inc(dv4.sem, 1)
                dv4.inc()
                for si, (h, r0, r1, c0, wid) in enumerate(slabs):
                    ld4.wait(vector, si, si // 2 + 1)
                    vector.tensor_reduce(abscol[0:(r1 - r0), si:si + 1],
                                         big_b[0:(r1 - r0), (si % 2) * 2816:(si % 2) * 2816 + wid],
                                         AX, ALU.add,
                                         apply_absolute_value=True).then_inc(dv4.sem, 1)
                    dv4.inc()
                vector.wait_ge(pe4.sem, 1)
                vector.tensor_reduce(stat[0:1, 0:1], bank[6][0:1, 0:4], AX, ALU.add)
                vector.tensor_reduce(stat[0:1, 1:2], bank[6][0:1, 4:8], AX, ALU.add)
                vector.tensor_reduce(stat[0:1, 2:3], bank[6][0:1, 8:14], AX,
                                     ALU.add).then_inc(dv4.sem, 1)
                dv4.inc()

            @b.tensor
            def _(tensor):
                tensor.wait_ge(dv4.sem, nsl + 1)
                tensor.matmul(bank[6][0:1, 0:16], onescol[:], abscol[:],
                              start=True, stop=True).then_inc(pe4.sem, 1)
                pe4.inc()

            @b.sync
            def _(sync):
                for si, (h, r0, r1, c0, wid) in enumerate(slabs):
                    if si >= 2:
                        sync.wait_ge(dv4.sem, si)
                    ld4.dma(sync, si, big_b[0:(r1 - r0), (si % 2) * 2816:(si % 2) * 2816 + wid],
                            h[r0:r1, c0:c0 + wid])
                sync.wait_ge(dv4.sem, nsl + 2)
                dma(sync, so_w, wabs_o[:, :], stat[0:1, 0:4])
                sync.wait_ge(so_x.sem, so_x.n)
                sync.wait_ge(so_xn.sem, so_xn.n)
                for cc in so_q.c + so_kv.c:
                    sync.wait_ge(cc.sem, cc.n)
                sync.wait_ge(so_k.sem, so_k.n)
                sync.wait_ge(so_w.sem, so_w.n)

    return nc


def shard_inputs_A(inputs, cores=range(8)):
    x = np.asarray(inputs["L0_attn_x"], np.float32).reshape(8192, DM)
    wqkv = np.ascontiguousarray(np.asarray(inputs["L0_attn_w_qkv"], np.float32))
    w1T = np.ascontiguousarray(np.asarray(inputs["L0_ffn_w1"], np.float32).T)
    w2T = np.ascontiguousarray(np.asarray(inputs["L0_ffn_w2"], np.float32).T)
    w3T = np.ascontiguousarray(np.asarray(inputs["L0_ffn_w3"], np.float32).T)
    C = host_constants()
    in_maps = []
    for c in cores:
        in_maps.append(dict(
            xsl=np.ascontiguousarray(x[c * TL:(c + 1) * TL]),
            wqkv=wqkv,
            w1sl=np.ascontiguousarray(w1T[c * 256:(c + 1) * 256]),
            w2sl=np.ascontiguousarray(w2T[c * 256:(c + 1) * 256]),
            w3sl=np.ascontiguousarray(w3T[c * 704:(c + 1) * 704]),
            ident=C["ident"], onescol=C["onescol"], onesrow=C["onesrow"],
        ))
    return in_maps


def run_A(inputs, cores=range(8), upto=4):
    nc = build_A(upto)
    res = run_bass_kernel_spmd(nc, shard_inputs_A(inputs, cores), core_ids=list(cores))
    return res.results


def build_B(upto=2):
    nc = bass.Bass("TRN2", target_bir_lowering=False)
    q_h = nc.dram_tensor("q_h", [512, 4096], F32, kind="ExternalInput")
    kv_h = nc.dram_tensor("kv_h", [512, 4096], F32, kind="ExternalInput")
    k_h = nc.dram_tensor("k_h", [512, 4096], F32, kind="ExternalInput")
    decay_sel = nc.dram_tensor("decay_sel", [1, 4], F32, kind="ExternalInput")
    wabs_all = nc.dram_tensor("wabs_all", [8, 4], F32, kind="ExternalInput")
    numel_row = nc.dram_tensor("numel_row", [1, 4], F32, kind="ExternalInput")
    w1sl = nc.dram_tensor("w1sl", [256, HFF], F32, kind="ExternalInput")
    w2sl = nc.dram_tensor("w2sl", [256, HFF], F32, kind="ExternalInput")
    w3sl = nc.dram_tensor("w3sl", [704, DM], F32, kind="ExternalInput")
    onescolD = nc.dram_tensor("onescol", [128, 1], F32, kind="ExternalInput")
    onescol8D = nc.dram_tensor("onescol8", [8, 1], F32, kind="ExternalInput")
    onesrowD = nc.dram_tensor("onesrow", [1, 128], F32, kind="ExternalInput")

    attn_o = nc.dram_tensor("attn_h", [512, 4096], F32, kind="ExternalOutput")
    w1q_o = nc.dram_tensor("w1q", [256, HFF], BF16, kind="ExternalOutput")
    w2q_o = nc.dram_tensor("w2q", [256, HFF], BF16, kind="ExternalOutput")
    w3q_o = nc.dram_tensor("w3q", [704, DM], BF16, kind="ExternalOutput")
    dbgB_o = nc.dram_tensor("dbgB", [1, 16], F32, kind="ExternalOutput")

    T4 = 4096
    with ExitStack() as es:
        E = es.enter_context
        big_b = E(nc.sbuf_tensor("big_b", [128, 2 * 2816], F32))
        qhi = E(nc.sbuf_tensor("qhi", [128, 2816], BF16))
        qlo = E(nc.sbuf_tensor("qlo", [128, 2816], BF16))
        wqout = E(nc.sbuf_tensor("wqout", [128, 2 * 2816], BF16))
        thr = E(nc.sbuf_tensor("thr", [128, 8], F32))
        wabs_sb = E(nc.sbuf_tensor("wabs_sb", [8, 4], F32))
        nm_sb = E(nc.sbuf_tensor("nm_sb", [1, 4], F32))
        grow = E(nc.sbuf_tensor("grow", [1, 4], F32))
        throw = E(nc.sbuf_tensor("throw", [1, 8], F32))
        dsel = E(nc.sbuf_tensor("dsel", [1, 4], F32))
        sigr = E(nc.sbuf_tensor("sigr", [1, 4], F32))
        dcol = E(nc.sbuf_tensor("dcol", [128, 4], F32))
        onescol = E(nc.sbuf_tensor("onescol_s", [128, 1], F32))
        onescol8 = E(nc.sbuf_tensor("onescol8_s", [8, 1], F32))
        onesrow = E(nc.sbuf_tensor("onesrow_s", [1, 128], F32))
        qt = E(nc.sbuf_tensor("qt", [128, T4], F32))
        kvt = E(nc.sbuf_tensor("kvt", [128, T4], F32))
        kt = E(nc.sbuf_tensor("kt", [128, T4], F32))
        afull = E(nc.sbuf_tensor("afull", [128, T4], F32))
        onesfull = E(nc.sbuf_tensor("onesfull", [128, T4], F32))
        kvs = E(nc.sbuf_tensor("kvs", [128, T4], F32))
        ks = E(nc.sbuf_tensor("ks", [128, T4], F32))
        attnst = E(nc.sbuf_tensor("attnst", [128, T4], F32))
        dn1 = E(nc.sbuf_tensor("dn1", [128, 512], F32))
        dn2 = E(nc.sbuf_tensor("dn2", [1, 512], F32))
        dn3 = E(nc.sbuf_tensor("dn3", [1, 512], F32))
        bank = [E(nc.psum_tensor(f"bank{i}", [128, 512], F32)) for i in range(8)]

        ldB = Ctr(nc, "ldB")
        ldS = SlotCtr(nc, "ldS", 2)
        dvB = Ctr(nc, "dvB")
        acB = Ctr(nc, "acB")
        peB = Ctr(nc, "peB")
        soB = SlotCtr(nc, "soB", 2)
        slabs = []
        for mi, wsl in ((0, w1sl), (1, w2sl)):
            for r0 in (0, 128):
                for c0 in (0, 2816):
                    slabs.append((mi, wsl, r0, 128, c0, 2816))
        for r0 in range(0, 704, 128):
            slabs.append((2, w3sl, r0, min(r0 + 128, 704) - r0, 0, DM))
        outm = {0: w1q_o, 1: w2q_o, 2: w3q_o}
        with nc.Block() as b:
            @b.sync
            def _(sync):
                dma(sync, ldB, wabs_sb[:], wabs_all[:, :])
                dma(sync, ldB, nm_sb[:], numel_row[:, :])
                dma(sync, ldB, onescol[:], onescolD[:, :])
                dma(sync, ldB, onescol8[:], onescol8D[:, :])
                dma(sync, ldB, onesrow[:], onesrowD[:, :])
                dma(sync, ldB, dsel[:], decay_sel[:, :])
                nsl = len(slabs)
                def store_slab(sj):
                    mi, h, r0, nr, c0, wid = slabs[sj]
                    sync.wait_ge(dvB.sem, 1 + 3 * (sj + 1))
                    soB.dma(sync, sj, outm[mi][r0:r0 + nr, c0:c0 + wid],
                            wqout[0:nr, (sj % 2) * 2816:(sj % 2) * 2816 + wid])
                for si, (mi, h, r0, nr, c0, wid) in enumerate(slabs):
                    if si >= 2:
                        sync.wait_ge(dvB.sem, 1 + 3 * (si - 1))
                    ldS.dma(sync, si, big_b[0:nr, (si % 2) * 2816:(si % 2) * 2816 + wid],
                            h[r0:r0 + nr, c0:c0 + wid])
                    if si >= 2:
                        store_slab(si - 2)
                store_slab(nsl - 2)
                store_slab(nsl - 1)

            @b.tensor
            def _(tensor):
                tensor.wait_ge(ldB.sem, 16 * 6)
                tensor.matmul(bank[0][0:1, 0:4], onescol8[:], wabs_sb[:],
                              start=True, stop=True).then_inc(peB.sem, 1)
                peB.inc()
                tensor.wait_ge(ldB.sem, 16 * 6)
                tensor.wait_ge(acB.sem, 2)
                tensor.matmul(bank[1][:, 0:8], onesrow[:], throw[0:1, 0:8],
                              start=True, stop=True).then_inc(peB.sem, 1)
                peB.inc()

            @b.vector
            def _(vector):
                vector.wait_ge(peB.sem, 1)
                vector.wait_ge(ldB.sem, 16 * 6)
                vector.tensor_tensor(grow[:], bank[0][0:1, 0:4], nm_sb[:],
                                     ALU.mult).then_inc(dvB.sem, 1)
                dvB.inc()
                vector.wait_ge(acB.sem, 3)
                for si, (mi, h, r0, nr, c0, wid) in enumerate(slabs):
                    ldS.wait(vector, si, si // 2 + 1)
                    if si >= 2:
                        soB.wait(vector, si, si // 2)
                    src = big_b[0:nr, (si % 2) * 2816:(si % 2) * 2816 + wid]
                    vector.tensor_scalar(qhi[0:nr, 0:wid], src, thr[0:nr, mi:mi + 1],
                                         None, ALU.is_gt)
                    vector.tensor_scalar(qlo[0:nr, 0:wid], src, thr[0:nr, 4 + mi:5 + mi],
                                         None, ALU.is_lt)
                    vector.tensor_tensor(wqout[0:nr, (si % 2) * 2816:(si % 2) * 2816 + wid],
                                         qhi[0:nr, 0:wid], qlo[0:nr, 0:wid],
                                         ALU.subtract).then_inc(dvB.sem, 3)
                    dvB.inc(3)

            @b.scalar
            def _(scalar):
                scalar.wait_ge(dvB.sem, 1)
                scalar.activation(throw[0:1, 0:4], grow[:], AF.Copy, bias=5e-9,
                                  scale=0.5).then_inc(acB.sem, 1)
                acB.inc()
                scalar.activation(throw[0:1, 4:8], grow[:], AF.Copy, bias=-5e-9,
                                  scale=-0.5).then_inc(acB.sem, 1)
                acB.inc()
                scalar.wait_ge(peB.sem, 2)
                scalar.copy(thr[:, 0:8], bank[1][:, 0:8]).then_inc(acB.sem, 1)
                acB.inc()
                scalar.wait_ge(ldB.sem, 16 * 6)
                scalar.activation(sigr[:], dsel[:], AF.Sigmoid).then_inc(acB.sem, 1)
                acB.inc()

        if upto < 2:
            with nc.Block() as b:
                @b.sync
                def _(sync):
                    dma(sync, ldB, dbgB_o[0:1, 0:4], grow[:])
                    dma(sync, ldB, dbgB_o[0:1, 8:16], throw[:])
                    sync.wait_ge(ldB.sem, ldB.n)
                    for cc in soB.c:
                        sync.wait_ge(cc.sem, cc.n)
            return nc
        ldU = Ctr(nc, "ldU")
        dvU = Ctr(nc, "dvU")
        acU = Ctr(nc, "acU")
        peU = Ctr(nc, "peU")
        evU = Ctr(nc, "evU")
        soU = Ctr(nc, "soU")
        with nc.Block() as b:
            @b.sync
            def _(sync):
                for u in range(4):
                    if u >= 1:
                        sync.wait_ge(evU.sem, 8 * u)
                    dma(sync, ldU, qt[:], q_h[u * 128:(u + 1) * 128, :])
                    dma(sync, ldU, kvt[:], kv_h[u * 128:(u + 1) * 128, :])
                    dma(sync, ldU, kt[:], k_h[u * 128:(u + 1) * 128, :])
                    sync.wait_ge(evU.sem, 8 * (u + 1))
                    dma(sync, soU, attn_o[u * 128:(u + 1) * 128, :], attnst[:])
                sync.wait_ge(soU.sem, 16 * 4)
                for cc in soB.c:
                    sync.wait_ge(cc.sem, cc.n)

            @b.vector
            def _(vector):
                vector.memset(onesfull[:], 1.0).then_inc(dvU.sem, 1)
                dvU.inc()
                vector.wait_ge(acU.sem, 1)
                for u in range(4):
                    vector.wait_ge(soU.sem, 16 * u)
                    vector.tensor_scalar_mul(afull[:], onesfull[:],
                                             dcol[:, u:u + 1]).then_inc(dvU.sem, 1)
                    dvU.inc()
                    vector.wait_ge(ldU.sem, 16 * (3 * u + 3))
                    vector.tensor_tensor_scan(kvs[:], afull[:], kvt[:], 0.0,
                                              ALU.mult, ALU.add).then_inc(dvU.sem, 1)
                    dvU.inc()
                    vector.tensor_tensor_scan(ks[:], afull[:], kt[:], 0.0,
                                              ALU.mult, ALU.add).then_inc(dvU.sem, 1)
                    dvU.inc()
                    for ch in range(8):
                        sl = slice(ch * 512, (ch + 1) * 512)
                        vector.tensor_tensor(dn1[:], qt[:, sl], ks[:, sl],
                                             ALU.mult).then_inc(dvU.sem, 1)
                        dvU.inc()
                        vector.wait_ge(peU.sem, 1 + u * 16 + 2 * ch + 1)
                        vector.tensor_scalar_max(dn2[:], bank[2 + ch % 2][0:1, :], 1e-6)
                        vector.reciprocal(dn3[:], dn2[:]).then_inc(dvU.sem, 1)
                        dvU.inc()
                        vector.wait_ge(peU.sem, 1 + u * 16 + 2 * ch + 2)
                        vector.tensor_tensor(dn1[:], qt[:, sl], kvs[:, sl], ALU.mult)
                        vector.tensor_tensor(attnst[:, sl], dn1[:],
                                             bank[4 + ch % 2][:, :],
                                             ALU.mult).then_inc(evU.sem, 1)
                        evU.inc()

            @b.scalar
            def _(scalar):
                scalar.wait_ge(peU.sem, 1)
                scalar.copy(dcol[:, 0:4], bank[6][:, 0:4]).then_inc(acU.sem, 1)
                acU.inc()

            @b.tensor
            def _(tensor):
                tensor.matmul(bank[6][:, 0:4], onesrow[:], sigr[0:1, 0:4],
                              start=True, stop=True).then_inc(peU.sem, 1)
                peU.inc()
                for u in range(4):
                    for ch in range(8):
                        tensor.wait_ge(dvU.sem, 1 + u * 19 + 3 + 2 * ch + 1)
                        tensor.matmul(bank[2 + ch % 2][0:1, :], onescol[:], dn1[:],
                                      start=True, stop=True).then_inc(peU.sem, 1)
                        peU.inc()
                        tensor.wait_ge(dvU.sem, 1 + u * 19 + 3 + 2 * ch + 2)
                        tensor.matmul(bank[4 + ch % 2][:, :], onesrow[:], dn3[:],
                                      start=True, stop=True).then_inc(peU.sem, 1)
                        peU.inc()

    return nc


def shard_inputs_B(resA, inputs, cores=range(8)):
    def glob(name):
        g = []
        for bb in range(2):
            g.append(np.concatenate([resA[4 * bb + s][name] for s in range(4)], axis=1))
        return g
    q_g = glob("q_fm"); kv_g = glob("kv_fm"); k_g = glob("k_fm")
    wabs_all = np.stack([resA[c]["wabs"][0] for c in range(8)], axis=0)
    decay = np.asarray(inputs["L0_attn_decay_param"], np.float32)
    w1T = np.ascontiguousarray(np.asarray(inputs["L0_ffn_w1"], np.float32).T)
    w2T = np.ascontiguousarray(np.asarray(inputs["L0_ffn_w2"], np.float32).T)
    w3T = np.ascontiguousarray(np.asarray(inputs["L0_ffn_w3"], np.float32).T)
    nm = np.array([[1.0 / w1T.size, 1.0 / w2T.size, 1.0 / w3T.size, 0.0]], np.float32)
    C = host_constants()
    in_maps = []
    for c in cores:
        bb = c // 4
        h0 = (4 * c) % 16
        in_maps.append(dict(
            q_h=np.ascontiguousarray(q_g[bb][h0 * 128:(h0 + 4) * 128]),
            kv_h=np.ascontiguousarray(kv_g[bb][h0 * 128:(h0 + 4) * 128]),
            k_h=np.ascontiguousarray(k_g[bb][h0 * 128:(h0 + 4) * 128]),
            decay_sel=np.ascontiguousarray(decay[h0:h0 + 4].reshape(1, 4)),
            wabs_all=wabs_all,
            numel_row=nm,
            w1sl=np.ascontiguousarray(w1T[c * 256:(c + 1) * 256]),
            w2sl=np.ascontiguousarray(w2T[c * 256:(c + 1) * 256]),
            w3sl=np.ascontiguousarray(w3T[c * 704:(c + 1) * 704]),
            onescol=C["onescol"], onesrow=C["onesrow"],
            onescol8=np.ones((8, 1), np.float32),
        ))
    return in_maps


def run_B(resA, inputs, cores=range(8), upto=2):
    nc = build_B(upto)
    res = run_bass_kernel_spmd(nc, shard_inputs_B(resA, inputs, cores), core_ids=list(cores))
    return res.results


def build_C():
    nc = bass.Bass("TRN2", target_bir_lowering=False)
    x_fm = nc.dram_tensor("x_fm", [DM, TL], F32, kind="ExternalInput")
    xn_bf = nc.dram_tensor("xn_bf", [DM, TL], BF16, kind="ExternalInput")
    attn_fm = nc.dram_tensor("attn_fm", [DM, TL], F32, kind="ExternalInput")
    w_out = nc.dram_tensor("w_out", [DM, DM], F32, kind="ExternalInput")
    w1qT = nc.dram_tensor("w1qT", [DM, HFF], BF16, kind="ExternalInput")
    w2qT = nc.dram_tensor("w2qT", [DM, HFF], BF16, kind="ExternalInput")
    w3qT = nc.dram_tensor("w3qT", [HFF, DM], BF16, kind="ExternalInput")
    resid1_o = nc.dram_tensor("resid1_fm", [DM, TL], F32, kind="ExternalOutput")

    NHB = 22
    with ExitStack() as es:
        E = es.enter_context
        bfA = E(nc.sbuf_tensor("bfA", [128, NCH * TL], BF16))
        hidbf = E(nc.sbuf_tensor("hidbf", [128, 44 * TL], BF16))
        g_fm = E(nc.sbuf_tensor("g_fm", [128, NCH * TL], BF16))
        w1b = [E(nc.sbuf_tensor(f"w1b{i}", [128, 256], BF16)) for i in range(2)]
        w2b = [E(nc.sbuf_tensor(f"w2b{i}", [128, 256], BF16)) for i in range(2)]
        w3b = [E(nc.sbuf_tensor(f"w3b{i}", [128, 512], BF16)) for i in range(2)]
        wob = [E(nc.sbuf_tensor(f"wob{i}", [128, 512], BF16)) for i in range(2)]
        sgt = E(nc.sbuf_tensor("sgt", [128, 2 * 512], F32))
        xb = E(nc.sbuf_tensor("xb", [128, TL], F32))
        rst = E(nc.sbuf_tensor("rst", [128, 2 * TL], F32))
        t1b = E(nc.sbuf_tensor("t1b", [128, 512], F32))
        bank = [E(nc.psum_tensor(f"bank{i}", [128, 512], F32)) for i in range(8)]

        ld1 = Ctr(nc, "ld1")
        wd1 = SlotCtr(nc, "wd1", 2)
        pc1 = Ctr(nc, "pc1")
        ev1 = Ctr(nc, "ev1")
        dv1 = Ctr(nc, "dv1")
        ac1 = Ctr(nc, "ac1")
        with nc.Block() as b:
            @b.sync
            def _(sync):
                dma(sync, ld1, fm_sb_ap(bfA, NCH), fm_hbm_ap(xn_bf))

            @b.gpsimd
            def _(gpsimd):
                for hb in range(NHB):
                    for kc in range(16):
                        i = hb * 16 + kc
                        if i >= 2:
                            gpsimd.wait_ge(pc1.sem, i - 1)
                        wd1.dma(gpsimd, i, w1b[i % 2][:],
                                w1qT[kc * 128:(kc + 1) * 128, hb * 256:(hb + 1) * 256])
                        wd1.dma(gpsimd, i, w2b[i % 2][:],
                                w2qT[kc * 128:(kc + 1) * 128, hb * 256:(hb + 1) * 256])

            @b.tensor
            def _(tensor):
                tensor.wait_ge(ld1.sem, 16)
                for hb in range(NHB):
                    tensor.wait_ge(ev1.sem, 4 * hb)
                    for kc in range(16):
                        i = hb * 16 + kc
                        wd1.wait(tensor, i, 2 * (i // 2 + 1))
                        for hc2 in range(2):
                            for tc in range(2):
                                tensor.matmul(bank[hc2 * 2 + tc][:, :],
                                              w1b[i % 2][:, hc2 * 128:(hc2 + 1) * 128],
                                              bfA[:, kc * TL + tc * 512: kc * TL + (tc + 1) * 512],
                                              start=(kc == 0), stop=(kc == 15))
                                mm = tensor.matmul(bank[4 + hc2 * 2 + tc][:, :],
                                              w2b[i % 2][:, hc2 * 128:(hc2 + 1) * 128],
                                              bfA[:, kc * TL + tc * 512: kc * TL + (tc + 1) * 512],
                                              start=(kc == 0), stop=(kc == 15))
                        mm.then_inc(pc1.sem, 1)
                        pc1.inc()

            @b.scalar
            def _(scalar):
                for hb in range(NHB):
                    for hc2 in range(2):
                        for tc in range(2):
                            j = hb * 4 + hc2 * 2 + tc
                            scalar.wait_ge(pc1.sem, 16 * (hb + 1))
                            if j >= 2:
                                scalar.wait_ge(ev1.sem, j - 1)
                            scalar.activation(sgt[:, (j % 2) * 512:(j % 2) * 512 + 512],
                                              bank[hc2 * 2 + tc][:, :],
                                              AF.Sigmoid).then_inc(ac1.sem, 1)
                            ac1.inc()

            @b.vector
            def _(vector):
                for hb in range(NHB):
                    for hc2 in range(2):
                        for tc in range(2):
                            j = hb * 4 + hc2 * 2 + tc
                            hc = hb * 2 + hc2
                            vector.wait_ge(ac1.sem, j + 1)
                            vector.tensor_tensor(
                                hidbf[:, hc * TL + tc * 512: hc * TL + (tc + 1) * 512],
                                sgt[:, (j % 2) * 512:(j % 2) * 512 + 512],
                                bank[4 + hc2 * 2 + tc][:, :],
                                ALU.mult).then_inc(ev1.sem, 1)
                            ev1.inc()

        wd2 = SlotCtr(nc, "wd2", 2)
        pc2 = Ctr(nc, "pc2")
        ev2 = Ctr(nc, "ev2")
        with nc.Block() as b:
            @b.gpsimd
            def _(gpsimd):
                for dblk in range(4):
                    for hk in range(44):
                        j = dblk * 44 + hk
                        if j >= 2:
                            gpsimd.wait_ge(pc2.sem, j - 1)
                        wd2.dma(gpsimd, j, w3b[j % 2][:],
                                w3qT[hk * 128:(hk + 1) * 128, dblk * 512:(dblk + 1) * 512])

            @b.tensor
            def _(tensor):
                for dblk in range(4):
                    tensor.wait_ge(ev2.sem, 8 * dblk)
                    for hk in range(44):
                        j = dblk * 44 + hk
                        wd2.wait(tensor, j, j // 2 + 1)
                        for dc4 in range(4):
                            for tc in range(2):
                                mm = tensor.matmul(
                                    bank[dc4 * 2 + tc][:, :],
                                    w3b[j % 2][:, dc4 * 128:(dc4 + 1) * 128],
                                    hidbf[:, hk * TL + tc * 512: hk * TL + (tc + 1) * 512],
                                    start=(hk == 0), stop=(hk == 43))
                        mm.then_inc(pc2.sem, 1)
                        pc2.inc()

            @b.scalar
            def _(scalar):
                for dblk in range(4):
                    scalar.wait_ge(pc2.sem, 44 * (dblk + 1))
                    for dc4 in range(4):
                        for tc in range(2):
                            dc = dblk * 4 + dc4
                            scalar.activation(
                                g_fm[:, dc * TL + tc * 512: dc * TL + (tc + 1) * 512],
                                bank[dc4 * 2 + tc][:, :],
                                AF.Silu).then_inc(ev2.sem, 1)
                            ev2.inc()

        ld3 = Ctr(nc, "ld3")
        ldat = Ctr(nc, "ldat")
        wd3 = SlotCtr(nc, "wd3", 2)
        pc3 = Ctr(nc, "pc3")
        ev3 = Ctr(nc, "ev3")
        dv3 = Ctr(nc, "dv3")
        so3 = SlotCtr(nc, "so3", 2)
        with nc.Block() as b:
            @b.gpsimd
            def _(gpsimd):
                dma(gpsimd, ldat, fm_sb_ap(bfA, NCH), fm_hbm_ap(attn_fm))
                for dblk in range(4):
                    for kc in range(16):
                        j = dblk * 16 + kc
                        if j >= 2:
                            gpsimd.wait_ge(pc3.sem, j - 1)
                        wd3.dma(gpsimd, j, wob[j % 2][:],
                                w_out[kc * 128:(kc + 1) * 128, dblk * 512:(dblk + 1) * 512])

            @b.sync
            def _(sync):
                for dc in range(16):
                    if dc >= 1:
                        sync.wait_ge(dv3.sem, 2 * dc)
                    dma(sync, ld3, xb[:, 0:TL],
                        x_fm[dc * 128:(dc + 1) * 128, :])
                    sync.wait_ge(dv3.sem, 2 * (dc + 1))
                    so3.dma(sync, dc, resid1_o[dc * 128:(dc + 1) * 128, :],
                            rst[:, (dc % 2) * TL:(dc % 2 + 1) * TL])
                for cc in so3.c:
                    sync.wait_ge(cc.sem, cc.n)

            @b.tensor
            def _(tensor):
                tensor.wait_ge(ldat.sem, 16)
                for dblk in range(4):
                    tensor.wait_ge(ev3.sem, 8 * dblk)
                    for kc in range(16):
                        j = dblk * 16 + kc
                        wd3.wait(tensor, j, j // 2 + 1)
                        for dc4 in range(4):
                            for tc in range(2):
                                mm = tensor.matmul(
                                    bank[dc4 * 2 + tc][:, :],
                                    wob[j % 2][:, dc4 * 128:(dc4 + 1) * 128],
                                    bfA[:, kc * TL + tc * 512: kc * TL + (tc + 1) * 512],
                                    start=(kc == 0), stop=(kc == 15))
                        mm.then_inc(pc3.sem, 1)
                        pc3.inc()

            @b.vector
            def _(vector):
                for dblk in range(4):
                    vector.wait_ge(pc3.sem, 16 * (dblk + 1))
                    for dc4 in range(4):
                        dc = dblk * 4 + dc4
                        for tc in range(2):
                            j2 = dc * 2 + tc
                            vector.wait_ge(ld3.sem, 16 * (dc + 1))
                            vector.tensor_tensor(t1b[:, 0:512],
                                                 bank[dc4 * 2 + tc][:, :],
                                                 g_fm[:, dc * TL + tc * 512: dc * TL + (tc + 1) * 512],
                                                 ALU.mult).then_inc(ev3.sem, 1)
                            ev3.inc()
                            if dc >= 2 and tc == 0:
                                so3.wait(vector, dc, dc // 2)
                            vector.tensor_tensor(
                                rst[:, (dc % 2) * TL + tc * 512:(dc % 2) * TL + (tc + 1) * 512],
                                t1b[:, 0:512],
                                xb[:, tc * 512:(tc + 1) * 512],
                                ALU.add).then_inc(dv3.sem, 1)
                            dv3.inc()

    return nc


def shard_inputs_C(resA, resB, inputs, cores=range(8)):
    import ml_dtypes
    def glob_attn():
        g = []
        for bb in range(2):
            g.append(np.concatenate([resB[4 * bb + s]["attn_h"] for s in range(4)], axis=0))
        return g
    attn_g = glob_attn()
    w1qT = np.concatenate([resB[c]["w1q"] for c in range(8)], axis=0)
    w2qT = np.concatenate([resB[c]["w2q"] for c in range(8)], axis=0)
    w3qT = np.concatenate([resB[c]["w3q"] for c in range(8)], axis=0)
    w_out = np.ascontiguousarray(np.asarray(inputs["L0_attn_w_out"], np.float32))
    in_maps = []
    for c in cores:
        bb, sl = c // 4, c % 4
        in_maps.append(dict(
            x_fm=resA[c]["x_fm"],
            xn_bf=resA[c]["xn_bf"],
            attn_fm=np.ascontiguousarray(attn_g[bb][:, sl * TL:(sl + 1) * TL]),
            w_out=w_out, w1qT=w1qT, w2qT=w2qT, w3qT=w3qT,
        ))
    return in_maps


def run_C(resA, resB, inputs, cores=range(8)):
    nc = build_C()
    res = run_bass_kernel_spmd(nc, shard_inputs_C(resA, resB, inputs, cores), core_ids=list(cores))
    return res.results


def build_D():
    nc = bass.Bass("TRN2", target_bir_lowering=False)
    resid1 = nc.dram_tensor("resid1_fm", [DM, TL], F32, kind="ExternalInput")
    WxT = nc.dram_tensor("WxT", [DM, 2304], F32, kind="ExternalInput")
    onescolD = nc.dram_tensor("onescol", [128, 1], F32, kind="ExternalInput")
    onesrowD = nc.dram_tensor("onesrow", [1, 128], F32, kind="ExternalInput")

    xn2f_o = nc.dram_tensor("xn2_fm", [DM, TL], F32, kind="ExternalOutput")
    xn2b_o = nc.dram_tensor("xn2_bf", [DM, TL], BF16, kind="ExternalOutput")
    dt_o = nc.dram_tensor("dt_fm", [DM, TL], F32, kind="ExternalOutput")
    brow_o = nc.dram_tensor("brow", [16, TL], F32, kind="ExternalOutput")
    crow_o = nc.dram_tensor("crow", [16, TL], F32, kind="ExternalOutput")

    with ExitStack() as es:
        E = es.enter_context
        rfm = E(nc.sbuf_tensor("rfm", [128, NCH * TL], F32))
        xn2bf = E(nc.sbuf_tensor("xn2bf", [128, NCH * TL], BF16))
        stage = E(nc.sbuf_tensor("stage", [128, 2 * TL], F32))
        sq = [E(nc.sbuf_tensor(f"sq{i}", [128, TL], F32)) for i in range(2)]
        stat = E(nc.sbuf_tensor("stat", [1, 4 * TL], F32))
        s1 = E(nc.sbuf_tensor("s1", [128, 2 * 512], F32))
        s2 = E(nc.sbuf_tensor("s2", [128, 2 * 512], F32))
        s3 = E(nc.sbuf_tensor("s3", [128, 2 * 512], F32))
        brow_sb = E(nc.sbuf_tensor("brow_sb", [16, TL], F32))
        crow_sb = E(nc.sbuf_tensor("crow_sb", [16, TL], F32))
        onescol = E(nc.sbuf_tensor("onescol_s", [128, 1], F32))
        onesrow = E(nc.sbuf_tensor("onesrow_s", [1, 128], F32))
        wxb = [E(nc.sbuf_tensor(f"wxb{i}", [128, 512], BF16)) for i in range(2)]
        tmp = E(nc.sbuf_tensor("tmp", [128, 512], F32))
        bank = [E(nc.psum_tensor(f"bank{i}", [128, 512], F32)) for i in range(8)]

        ld = Ctr(nc, "ld")
        ac1 = Ctr(nc, "ac1")
        pe1 = Ctr(nc, "pe1")
        dv1 = Ctr(nc, "dv1")
        po1 = Ctr(nc, "po1")
        soxf = SlotCtr(nc, "soxf", 2)
        soxb = Ctr(nc, "soxb")
        M, V, R, W = 0, TL, 2 * TL, 3 * TL
        with nc.Block() as b:
            @b.sync
            def _(sync):
                dma(sync, ld, onescol[:], onescolD[:, :])
                dma(sync, ld, onesrow[:], onesrowD[:, :])
                dma(sync, ld, fm_sb_ap(rfm, NCH), fm_hbm_ap(resid1))
                for dc in range(NCH):
                    sync.wait_ge(dv1.sem, 8 + 2 * (dc + 1))
                    soxf.dma(sync, dc, xn2f_o[dc * 128:(dc + 1) * 128, :],
                             stage[:, (dc % 2) * TL:(dc % 2 + 1) * TL])
                sync.wait_ge(po1.sem, 32)
                dma(sync, soxb, fm_hbm_ap(xn2b_o), fm_sb_ap(xn2bf, NCH))

            @b.scalar
            def _(scalar):
                scalar.wait_ge(ld.sem, 48)
                for dc in range(NCH):
                    if dc >= 2:
                        scalar.wait_ge(pe1.sem, 4 * (dc - 1))
                    scalar.activation(sq[dc % 2][:], rfm[:, dc * TL:(dc + 1) * TL],
                                      AF.Square).then_inc(ac1.sem, 1)
                    ac1.inc()
                scalar.wait_ge(dv1.sem, 7)
                scalar.activation(stat[0:1, W:W + TL], stat[0:1, R:R + TL],
                                  AF.Sqrt).then_inc(ac1.sem, 1)
                ac1.inc()

            @b.tensor
            def _(tensor):
                tensor.wait_ge(ld.sem, 48)
                for dc in range(NCH):
                    tensor.wait_ge(ac1.sem, dc + 1)
                    st_, sp_ = (dc == 0), (dc == NCH - 1)
                    for tc in range(2):
                        tensor.matmul(bank[0 + tc][0:1, :], onescol[:],
                                      rfm[:, dc * TL + tc * 512: dc * TL + (tc + 1) * 512],
                                      start=st_, stop=sp_).then_inc(pe1.sem, 1)
                        pe1.inc()
                        tensor.matmul(bank[6 + tc][0:1, :], onescol[:],
                                      sq[dc % 2][:, tc * 512:(tc + 1) * 512],
                                      start=st_, stop=sp_).then_inc(pe1.sem, 1)
                        pe1.inc()
                tensor.wait_ge(dv1.sem, 8)
                for tc in range(2):
                    tensor.matmul(bank[2 + tc][:, :], onesrow[:],
                                  stat[0:1, M + tc * 512: M + (tc + 1) * 512],
                                  start=True, stop=True).then_inc(pe1.sem, 1)
                    pe1.inc()
                    tensor.matmul(bank[4 + tc][:, :], onesrow[:],
                                  stat[0:1, V + tc * 512: V + (tc + 1) * 512],
                                  start=True, stop=True).then_inc(pe1.sem, 1)
                    pe1.inc()

            @b.vector
            def _(vector):
                vector.wait_ge(pe1.sem, 64)
                vector.tensor_scalar_mul(stat[0:1, M:M + 512], bank[0][0:1, :],
                                         1.0 / DM).then_inc(dv1.sem, 1)
                vector.tensor_scalar_mul(stat[0:1, M + 512:M + TL], bank[1][0:1, :],
                                         1.0 / DM).then_inc(dv1.sem, 1)
                vector.tensor_scalar_mul(stat[0:1, V:V + 512], bank[6][0:1, :],
                                         1.0 / DM).then_inc(dv1.sem, 1)
                vector.tensor_scalar_mul(stat[0:1, V + 512:V + TL], bank[7][0:1, :],
                                         1.0 / DM).then_inc(dv1.sem, 1)
                vector.tensor_mul(stat[0:1, R:R + TL], stat[0:1, M:M + TL],
                                  stat[0:1, M:M + TL]).then_inc(dv1.sem, 1)
                vector.tensor_sub(stat[0:1, W:W + TL], stat[0:1, V:V + TL],
                                  stat[0:1, R:R + TL]).then_inc(dv1.sem, 1)
                vector.tensor_scalar_add(stat[0:1, R:R + TL], stat[0:1, W:W + TL],
                                         1e-5).then_inc(dv1.sem, 1)
                dv1.inc(7)
                vector.wait_ge(ac1.sem, 17)
                vector.reciprocal(stat[0:1, V:V + TL],
                                  stat[0:1, W:W + TL]).then_inc(dv1.sem, 1)
                dv1.inc()
                vector.wait_ge(pe1.sem, 68)
                for dc in range(NCH):
                    if dc >= 2:
                        soxf.wait(vector, dc, dc // 2)
                        vector.wait_ge(po1.sem, 2 * (dc - 1))
                    for tc in range(2):
                        sl = slice(dc * TL + tc * 512, dc * TL + (tc + 1) * 512)
                        ssl = slice((dc % 2) * TL + tc * 512, (dc % 2) * TL + (tc + 1) * 512)
                        vector.tensor_tensor(tmp[:], rfm[:, sl],
                                             bank[2 + tc][:, :], ALU.subtract)
                        vector.tensor_tensor(stage[:, ssl], tmp[:],
                                             bank[4 + tc][:, :],
                                             ALU.mult).then_inc(dv1.sem, 1)
                        dv1.inc()

            @b.gpsimd
            def _(gpsimd):
                for dc in range(NCH):
                    for tc in range(2):
                        gpsimd.wait_ge(dv1.sem, 8 + dc * 2 + tc + 1)
                        sl = slice(dc * TL + tc * 512, dc * TL + (tc + 1) * 512)
                        ssl = slice((dc % 2) * TL + tc * 512, (dc % 2) * TL + (tc + 1) * 512)
                        gpsimd.tensor_copy(xn2bf[:, sl], stage[:, ssl]).then_inc(po1.sem, 1)
                        po1.inc()

        wd2 = SlotCtr(nc, "wd2", 2)
        pc2 = Ctr(nc, "pc2")
        ev2 = Ctr(nc, "ev2")
        ac2 = Ctr(nc, "ac2")
        sod = SlotCtr(nc, "sod", 2)
        sobc = Ctr(nc, "sobc")
        fblocks = [(fb, 4) for fb in range(4)] + [(4, 2)]
        with nc.Block() as b:
            @b.gpsimd
            def _(gpsimd):
                j = 0
                for (fb, nchk) in fblocks:
                    for kc in range(16):
                        if j >= 2:
                            gpsimd.wait_ge(pc2.sem, j - 1)
                        wd2.dma(gpsimd, j, wxb[j % 2][0:128, 0:nchk * 128],
                                WxT[kc * 128:(kc + 1) * 128,
                                    fb * 512:fb * 512 + nchk * 128])
                        j += 1

            @b.tensor
            def _(tensor):
                j = 0
                npieces = 0
                for (fb, nchk) in fblocks:
                    tensor.wait_ge(ev2.sem, npieces)
                    for kc in range(16):
                        wd2.wait(tensor, j, j // 2 + 1)
                        for fc in range(nchk):
                            for tc in range(2):
                                mm = tensor.matmul(
                                    bank[fc * 2 + tc][:, :],
                                    wxb[j % 2][:, fc * 128:(fc + 1) * 128],
                                    xn2bf[:, kc * TL + tc * 512: kc * TL + (tc + 1) * 512],
                                    start=(kc == 0), stop=(kc == 15))
                        mm.then_inc(pc2.sem, 1)
                        pc2.inc()
                        j += 1
                    npieces += nchk * 2

            @b.scalar
            def _(scalar):
                jj = 0
                for (fb, nchk) in fblocks:
                    scalar.wait_ge(pc2.sem, 16 * (fb + 1))
                    for fc in range(nchk):
                        fcg = fb * 4 + fc
                        for tc in range(2):
                            if fcg < 16:
                                sl2 = slice((jj % 2) * 512, (jj % 2) * 512 + 512)
                                if jj >= 2:
                                    scalar.wait_ge(ev2.sem, jj - 1)
                                scalar.activation(s1[:, sl2], bank[fc * 2 + tc][:, :],
                                                  AF.Abs)
                                scalar.activation(s2[:, sl2], s1[:, sl2], AF.Exp,
                                                  scale=-1.0)
                                scalar.activation(s3[:, sl2], s2[:, sl2], AF.Ln,
                                                  bias=1.0).then_inc(ac2.sem, 1)
                                ac2.inc()
                                jj += 1
                            else:
                                dstt = brow_sb if fcg == 16 else crow_sb
                                scalar.activation(dstt[0:16, tc * 512:(tc + 1) * 512],
                                                  bank[fc * 2 + tc][0:16, :],
                                                  AF.Copy).then_inc(ac2.sem, 1)
                                ac2.inc()

            @b.vector
            def _(vector):
                jj = 0
                nacc = 0
                for (fb, nchk) in fblocks:
                    for fc in range(nchk):
                        fcg = fb * 4 + fc
                        for tc in range(2):
                            nacc += 1
                            if fcg < 16:
                                sl2 = slice((jj % 2) * 512, (jj % 2) * 512 + 512)
                                vector.wait_ge(ac2.sem, nacc)
                                if fcg >= 2 and tc == 0:
                                    sod.wait(vector, fcg, fcg // 2)
                                if fcg == 0 and tc == 0:
                                    for cc in soxf.c:
                                        vector.wait_ge(cc.sem, cc.n)
                                vector.scalar_tensor_tensor(
                                    stage[:, (fcg % 2) * TL + tc * 512:(fcg % 2) * TL + (tc + 1) * 512],
                                    bank[fc * 2 + tc][:, :], 0.0, s3[:, sl2],
                                    ALU.max, ALU.add).then_inc(ev2.sem, 1)
                                ev2.inc()
                                jj += 1
                            else:
                                vector.wait_ge(ac2.sem, nacc)
                                vector.tensor_copy(tmp[0:16, 0:1],
                                                   bank[fc * 2 + tc][0:16, 0:1]).then_inc(ev2.sem, 1)
                                ev2.inc()

            @b.sync
            def _(sync):
                for fcg in range(16):
                    sync.wait_ge(ev2.sem, 2 * (fcg + 1))
                    sod.dma(sync, fcg, dt_o[fcg * 128:(fcg + 1) * 128, :],
                            stage[:, (fcg % 2) * TL:(fcg % 2 + 1) * TL])
                sync.wait_ge(ev2.sem, 36)
                dma(sync, sobc, brow_o[:, :], brow_sb[:])
                dma(sync, sobc, crow_o[:, :], crow_sb[:])
                sync.wait_ge(sobc.sem, sobc.n)
                for cc in sod.c:
                    sync.wait_ge(cc.sem, cc.n)
                sync.wait_ge(soxb.sem, soxb.n)

    return nc


def shard_inputs_D(resC, inputs, cores=range(8)):
    WxT = np.zeros((2048, 2304), np.float32)
    W = np.asarray(inputs["L1_attn_W_xproj"], np.float32)
    WxT[:, 0:2048] = W[0:2048].T
    WxT[:, 2048:2064] = W[2048:2064].T
    WxT[:, 2176:2192] = W[2064:2080].T
    C = host_constants()
    in_maps = []
    for c in cores:
        in_maps.append(dict(
            resid1_fm=resC[c]["resid1_fm"],
            WxT=WxT, onescol=C["onescol"], onesrow=C["onesrow"],
        ))
    return in_maps


def run_D(resC, inputs, cores=range(8)):
    nc = build_D()
    res = run_bass_kernel_spmd(nc, shard_inputs_D(resC, inputs, cores), core_ids=list(cores))
    return res.results


def build_E():
    nc = bass.Bass("TRN2", target_bir_lowering=False)
    T4 = 4096
    dt_sl = nc.dram_tensor("dt_sl", [2, 256, T4], F32, kind="ExternalInput")
    xn2_sl = nc.dram_tensor("xn2_sl", [2, 256, T4], F32, kind="ExternalInput")
    ball = nc.dram_tensor("ball", [2, 16, T4], F32, kind="ExternalInput")
    call = nc.dram_tensor("call", [2, 16, T4], F32, kind="ExternalInput")
    alog2 = nc.dram_tensor("alog2", [128, 32], F32, kind="ExternalInput")
    d2 = nc.dram_tensor("d2", [128, 2], F32, kind="ExternalInput")
    oh16D = nc.dram_tensor("oh16", [16, 16 * 128], F32, kind="ExternalInput")
    y_o = nc.dram_tensor("y_sl", [2, 256, T4], F32, kind="ExternalOutput")

    with ExitStack() as es:
        E = es.enter_context
        dtt = E(nc.sbuf_tensor("dtt", [128, T4], F32))
        xnt = E(nc.sbuf_tensor("xnt", [128, T4], F32))
        dtx = E(nc.sbuf_tensor("dtx", [128, T4], F32))
        at = [E(nc.sbuf_tensor(f"at{i}", [128, T4], F32)) for i in range(2)]
        bt = E(nc.sbuf_tensor("bt", [128, T4], F32))
        ht = E(nc.sbuf_tensor("ht", [128, T4], F32))
        yt = E(nc.sbuf_tensor("yt", [128, T4], F32))
        hct = E(nc.sbuf_tensor("hct", [128, 2 * 512], F32))
        Bt = [E(nc.sbuf_tensor(f"Bt{i}", [16, T4], F32)) for i in range(2)]
        Ct = [E(nc.sbuf_tensor(f"Ct{i}", [16, T4], F32)) for i in range(2)]
        At = E(nc.sbuf_tensor("At", [128, 32], F32))
        Ae = E(nc.sbuf_tensor("Ae", [128, 32], F32))
        Dt = E(nc.sbuf_tensor("Dt", [128, 2], F32))
        oh16 = E(nc.sbuf_tensor("oh16_s", [16, 16 * 128], F32))
        bank = [E(nc.psum_tensor(f"bank{i}", [128, 512], F32)) for i in range(4)]

        ld = Ctr(nc, "ld")
        ldu = SlotCtr(nc, "ldu", 2)
        dv = Ctr(nc, "dv")
        ac = Ctr(nc, "ac")
        pe = Ctr(nc, "pe")
        po = Ctr(nc, "po")
        soy = SlotCtr(nc, "soy", 2)

        with nc.Block() as b:
            @b.sync
            def _(sync):
                dma(sync, ld, At[:], alog2[:, :])
                dma(sync, ld, Dt[:], d2[:, :])
                dma(sync, ld, oh16[:], oh16D[:, :])
                for sq_ in range(2):
                    dma(sync, ld, Bt[sq_][:], ball[sq_, :, :])
                    dma(sync, ld, Ct[sq_][:], call[sq_, :, :])

            @b.scalar
            def _(scalar):
                scalar.wait_ge(ld.sem, ld.n)
                scalar.activation(Ae[:], At[:], AF.Exp).then_inc(ac.sem, 1)
                ac.inc()

            @b.vector
            def _(vector):
                vector.wait_ge(ac.sem, 1)
                vector.tensor_scalar_mul(At[:], Ae[:], -1.0).then_inc(dv.sem, 1)
                dv.inc()

        dv_n = [dv.n]
        ac_n = [ac.n]
        pe_n = [pe.n]
        po_n = [po.n]
        with nc.Block() as b:
            units = [(sq_, grp) for sq_ in range(2) for grp in range(2)]

            @b.sync
            def _(sync):
                for u, (sq_, grp) in enumerate(units):
                    if u >= 1:
                        sync.wait_ge(ac.sem, 1 + 16 * u)
                        sync.wait_ge(dv.sem, 1 + 274 * u)
                    ldu.dma(sync, u, dtt[:], dt_sl[sq_, grp * 128:(grp + 1) * 128, :])
                    ldu.dma(sync, u, xnt[:], xn2_sl[sq_, grp * 128:(grp + 1) * 128, :])
                    sync.wait_ge(po.sem, 128 * (u + 1))
                    soy.dma(sync, u, y_o[sq_, grp * 128:(grp + 1) * 128, :], yt[:])
                for cc in soy.c:
                    sync.wait_ge(cc.sem, cc.n)

            @b.scalar
            def _(scalar):
                for u, (sq_, grp) in enumerate(units):
                    ldu.wait(scalar, u, 2 * (u // 2 + 1))
                    scalar.wait_ge(dv.sem, 1)
                    for n in range(16):
                        if n >= 2:
                            scalar.wait_ge(dv.sem, 1 + 274 * u + 2 + 17 * (n - 2) + 9)
                        scalar.activation(at[n % 2][:], dtt[:], AF.Exp,
                                          scale=At[:, grp * 16 + n: grp * 16 + n + 1]
                                          ).then_inc(ac.sem, 1)
                        ac.inc()

            @b.tensor
            def _(tensor):
                tensor.wait_ge(ld.sem, ld.n)
                for u, (sq_, grp) in enumerate(units):
                    base = 1 + 274 * u
                    for n in range(16):
                        for ch in range(8):
                            if (n, ch) != (0, 0) and (n, ch) != (0, 1):
                                if ch >= 2:
                                    tensor.wait_ge(dv.sem, base + 2 + 17 * n + ch - 1)
                                else:
                                    tensor.wait_ge(dv.sem, base + 2 + 17 * (n - 1) + 6 + ch + 1)
                            tensor.matmul(bank[ch % 2][:, :],
                                          oh16[:, n * 128:(n + 1) * 128],
                                          Bt[sq_][:, ch * 512:(ch + 1) * 512],
                                          start=True, stop=True).then_inc(pe.sem, 1)
                            pe.inc()
                        for ch in range(8):
                            if (n, ch) != (0, 0) and (n, ch) != (0, 1):
                                if ch >= 2:
                                    tensor.wait_ge(dv.sem, base + 2 + 17 * n + 9 + ch - 1)
                                else:
                                    tensor.wait_ge(dv.sem, base + 2 + 17 * (n - 1) + 9 + 6 + ch + 1)
                            tensor.matmul(bank[2 + ch % 2][:, :],
                                          oh16[:, n * 128:(n + 1) * 128],
                                          Ct[sq_][:, ch * 512:(ch + 1) * 512],
                                          start=True, stop=True).then_inc(pe.sem, 1)
                            pe.inc()

            @b.vector
            def _(vector):
                for u, (sq_, grp) in enumerate(units):
                    base = 1 + 274 * u
                    ldu.wait(vector, u, 2 * (u // 2 + 1))
                    vector.tensor_tensor(dtx[:], dtt[:], xnt[:],
                                         ALU.mult).then_inc(dv.sem, 1)
                    dv.inc()
                    if u >= 1:
                        soy.wait(vector, u - 1, (u - 1) // 2 + 1)
                    vector.tensor_scalar_mul(yt[:], xnt[:],
                                             Dt[:, grp:grp + 1]).then_inc(dv.sem, 1)
                    dv.inc()
                    for n in range(16):
                        for ch in range(8):
                            vector.wait_ge(pe.sem, 256 * u + 16 * n + ch + 1)
                            vector.tensor_tensor(bt[:, ch * 512:(ch + 1) * 512],
                                                 dtx[:, ch * 512:(ch + 1) * 512],
                                                 bank[ch % 2][:, :],
                                                 ALU.mult).then_inc(dv.sem, 1)
                            dv.inc()
                        vector.wait_ge(ac.sem, 1 + 16 * u + n + 1)
                        vector.tensor_tensor_scan(ht[:], at[n % 2][:], bt[:], 0.0,
                                                  ALU.mult, ALU.add).then_inc(dv.sem, 1)
                        dv.inc()
                        for ch in range(8):
                            vector.wait_ge(pe.sem, 256 * u + 16 * n + 8 + ch + 1)
                            if not (n == 0 and ch < 2):
                                if ch >= 2:
                                    vector.wait_ge(po.sem, 128 * u + 8 * n + ch - 1)
                                else:
                                    vector.wait_ge(po.sem, 128 * u + 8 * (n - 1) + 6 + ch + 1)
                            vector.tensor_tensor(hct[:, (ch % 2) * 512:(ch % 2) * 512 + 512],
                                                 ht[:, ch * 512:(ch + 1) * 512],
                                                 bank[2 + ch % 2][:, :],
                                                 ALU.mult).then_inc(dv.sem, 1)
                            dv.inc()

            @b.gpsimd
            def _(gpsimd):
                for u, (sq_, grp) in enumerate(units):
                    base = 1 + 274 * u
                    for n in range(16):
                        for ch in range(8):
                            hc_done = base + 2 + 17 * n + 9 + ch + 1
                            gpsimd.wait_ge(dv.sem, hc_done)
                            gpsimd.tensor_tensor(yt[:, ch * 512:(ch + 1) * 512],
                                                 yt[:, ch * 512:(ch + 1) * 512],
                                                 hct[:, (ch % 2) * 512:(ch % 2) * 512 + 512],
                                                 ALU.add).then_inc(po.sem, 1)
                            po.inc()

    return nc


def shard_inputs_E(resD, inputs, cores=range(8)):
    def glob(name):
        out = np.empty((2, 2048, 4096), np.float32)
        for c in range(8):
            bb, sl = c // 4, c % 4
            out[bb, :, sl * TL:(sl + 1) * TL] = resD[c][name]
        return out
    dt_g = glob("dt_fm")
    xn2_g = glob("xn2_fm")
    ball = np.empty((2, 16, 4096), np.float32)
    call = np.empty((2, 16, 4096), np.float32)
    for c in range(8):
        bb, sl = c // 4, c % 4
        ball[bb, :, sl * TL:(sl + 1) * TL] = resD[c]["brow"]
        call[bb, :, sl * TL:(sl + 1) * TL] = resD[c]["crow"]
    A_log = np.asarray(inputs["L1_attn_A_log"], np.float32)
    Dv = np.asarray(inputs["L1_attn_D"], np.float32)
    C = host_constants()
    in_maps = []
    for c in cores:
        r0 = c * 256
        al = A_log[r0:r0 + 256].reshape(2, 128, 16)
        alog2 = np.ascontiguousarray(np.concatenate([al[0], al[1]], axis=1))
        d2 = np.ascontiguousarray(Dv[r0:r0 + 256].reshape(2, 128).T)
        in_maps.append(dict(
            dt_sl=np.ascontiguousarray(dt_g[:, r0:r0 + 256, :]),
            xn2_sl=np.ascontiguousarray(xn2_g[:, r0:r0 + 256, :]),
            ball=ball, call=call, alog2=alog2, d2=d2, oh16=C["oh16"],
        ))
    return in_maps


def run_E(resD, inputs, cores=range(8)):
    nc = build_E()
    res = run_bass_kernel_spmd(nc, shard_inputs_E(resD, inputs, cores), core_ids=list(cores))
    return res.results


MOE_NEXP = NEXP


def _moe_schedule():
    HALVES = [(0, 768, 6), (768, 640, 5)]
    MOE_TRUNC = ""
    groups = []
    for e in range(MOE_NEXP):
        last = (e == MOE_NEXP - 1) and MOE_TRUNC
        for which in (0, 1):
            if last and MOE_TRUNC == "gate" and which == 1:
                continue
            for hi, (off, fw, nfc) in enumerate(HALVES):
                for fc in range(nfc):
                    fcg = off // 128 + fc
                    groups.append(dict(kind=("gate" if which == 0 else "up"),
                                       e=e, hi=hi, off=off, fw=fw, fc=fc, fcg=fcg,
                                       banks=[(fc % 2) * 2 + tc for tc in range(2)]))
        if last and MOE_TRUNC in ("gate", "up"):
            continue
        for dblk in range(4):
            for fk in range(11):
                groups.append(dict(kind="down", e=e, dblk=dblk, fk=fk,
                                   banks=list(range(8))))
    pc_idx = {}
    done = {}
    evg = 0
    evd = 0
    for gi, g in enumerate(groups):
        pc_idx[gi] = gi + 1
        if g["kind"] == "gate":
            evg += 2
            done[gi] = ("g", evg)
        elif g["kind"] == "up":
            evd += 2
            done[gi] = ("d", evd)
        else:
            if g["fk"] == 10:
                evd += 8
                done[gi] = ("d", evd)
            else:
                done[gi] = None
    bank_writer = {}
    bank_wait = {}
    for gi, g in enumerate(groups):
        w = {}
        for bk in g["banks"]:
            if bk in bank_writer:
                ctr, v = done[bank_writer[bk]]
                w[ctr] = max(w.get(ctr, 0), v)
        if g["kind"] == "down" and g["fk"] > 0:
            w = {}
        bank_wait[gi] = w
        if g["kind"] == "down":
            if g["fk"] == 10:
                for bk in range(8):
                    bank_writer[bk] = gi
        else:
            for bk in g["banks"]:
                bank_writer[bk] = gi
    return groups, pc_idx, done, bank_wait


def build_F(upto=3):
    nc = bass.Bass("TRN2", target_bir_lowering=False)
    resid1 = nc.dram_tensor("resid1_fm", [DM, TL], F32, kind="ExternalInput")
    xn2b_i = nc.dram_tensor("xn2_bf", [DM, TL], BF16, kind="ExternalInput")
    y_fm = nc.dram_tensor("y_fm", [DM, TL], F32, kind="ExternalInput")
    gw2 = nc.dram_tensor("gw2", [128, 128], F32, kind="ExternalInput")
    w_gate = nc.dram_tensor("w_gate", [NEXP, DM, FEXP], F32, kind="ExternalInput")
    w_up = nc.dram_tensor("w_up", [NEXP, DM, FEXP], F32, kind="ExternalInput")
    w_down = nc.dram_tensor("w_down", [NEXP, FEXP, DM], F32, kind="ExternalInput")
    identD = nc.dram_tensor("ident", [128, 128], F32, kind="ExternalInput")
    oh8D = nc.dram_tensor("oh8", [8, 8 * 128], BF16, kind="ExternalInput")
    xout_o = nc.dram_tensor("xout", [TL, DM], F32, kind="ExternalOutput")
    fdbg_o = nc.dram_tensor("fdbg", [128, 64 + 8 * TL], F32, kind="ExternalOutput")

    HALVES = [(0, 768, 6), (768, 640, 5)]
    groups, pc_idx, done, bank_wait = _moe_schedule()
    gl = {}
    for gi, g in enumerate(groups):
        if g["kind"] == "down":
            gl[("down", g["e"], g["dblk"], g["fk"])] = gi
        else:
            gl[(g["kind"], g["e"], g["fcg"])] = gi

    with ExitStack() as es:
        E = es.enter_context
        xn2bf = E(nc.sbuf_tensor("xn2bf", [128, NCH * TL], BF16))
        wtebc = E(nc.sbuf_tensor("wtebc", [128, NEXP * TL], BF16))
        gsb = E(nc.sbuf_tensor("gsb", [128, 11 * TL], BF16))
        hbf = E(nc.sbuf_tensor("hbf", [128, 11 * TL], BF16))
        wgb = E(nc.sbuf_tensor("wgb", [128, 16 * 768], BF16))
        wdb = [E(nc.sbuf_tensor(f"wdb{i}", [128, 512], BF16)) for i in range(2)]
        moe = E(nc.sbuf_tensor("moe", [128, NCH * TL], F32))
        gwb = E(nc.sbuf_tensor("gwb", [128, 128], BF16))
        lgfm = moe
        lgtm = E(nc.sbuf_tensor("lgtm", [128, 64], F32))
        etm = E(nc.sbuf_tensor("etm", [128, 64], F32))
        tmp64 = E(nc.sbuf_tensor("tmp64", [128, 64], F32))
        wtetm = E(nc.sbuf_tensor("wtetm", [128, 64], F32))
        wtefm = E(nc.sbuf_tensor("wtefm", [8, TL], BF16))
        nmx = E(nc.sbuf_tensor("nmx", [128, 8], F32))
        m2c = E(nc.sbuf_tensor("m2c", [128, 8], F32))
        dnc = E(nc.sbuf_tensor("dnc", [128, 8], F32))
        rdc = E(nc.sbuf_tensor("rdc", [128, 8], F32))
        ident = E(nc.sbuf_tensor("ident_s", [128, 128], F32))
        oh8 = E(nc.sbuf_tensor("oh8_s", [8, 8 * 128], BF16))
        yb = E(nc.sbuf_tensor("yb", [128, TL], F32))
        rb = E(nc.sbuf_tensor("rb", [128, TL], F32))
        gst = E(nc.sbuf_tensor("gst", [128, TL], BF16))
        xos = E(nc.sbuf_tensor("xos", [128, TL], F32))
        outst = E(nc.sbuf_tensor("outst", [128, 2 * 128], F32))
        t5 = E(nc.sbuf_tensor("t5", [128, 512], BF16))
        bank = [E(nc.psum_tensor(f"bank{i}", [128, 512], F32)) for i in range(8)]

        ld = Ctr(nc, "ld")
        ldg = Ctr(nc, "ldg")

        pe1 = Ctr(nc, "pe1")
        ac1 = Ctr(nc, "ac1")
        dv1 = Ctr(nc, "dv1")
        po1 = Ctr(nc, "po1")
        with nc.Block() as b:
            @b.sync
            def _(sync):
                dma(sync, ld, ident[:], identD[:, :])
                dma(sync, ld, oh8[:], oh8D[:, :])
                dma(sync, ld, fm_sb_ap(xn2bf, NCH), fm_hbm_ap(xn2b_i))

            @b.gpsimd
            def _(gpsimd):
                dma(gpsimd, ldg, gwb[:], gw2[:, :])
                for tb in range(8):
                    sl8 = slice(tb * 8, (tb + 1) * 8)
                    gpsimd.wait_ge(dv1.sem, 8 + tb + 1)
                    gpsimd.tensor_tensor(tmp64[:, sl8], tmp64[:, sl8], etm[:, sl8],
                                         ALU.add).then_inc(po1.sem, 1)
                    po1.inc()
                gpsimd.wait_ge(dv1.sem, 24)
                gpsimd.tensor_scalar_add(dnc[:], m2c[:], 1.0).then_inc(po1.sem, 1)
                po1.inc()
                for tb in range(8):
                    sl8 = slice(tb * 8, (tb + 1) * 8)
                    gpsimd.tensor_scalar(wtetm[:, sl8], etm[:, sl8], m2c[:, tb:tb + 1],
                                         None, ALU.is_ge).then_inc(po1.sem, 1)
                    po1.inc()
                for tb in range(8):
                    sl8 = slice(tb * 8, (tb + 1) * 8)
                    gpsimd.wait_ge(dv1.sem, 25 + tb + 1)
                    gpsimd.tensor_scalar_mul(wtetm[:, sl8], wtetm[:, sl8],
                                             rdc[:, tb:tb + 1]).then_inc(po1.sem, 1)
                    po1.inc()

            @b.tensor
            def _(tensor):
                tensor.wait_ge(ld.sem, 48)
                tensor.wait_ge(ldg.sem, 16)
                mm = None
                for kc in range(16):
                    for tc in range(2):
                        mm = tensor.matmul(bank[tc][0:8, :], gwb[:, kc * 8:(kc + 1) * 8],
                                           xn2bf[:, kc * TL + tc * 512: kc * TL + (tc + 1) * 512],
                                           start=(kc == 0), stop=(kc == 15))
                mm.then_inc(pe1.sem, 1)
                pe1.inc()
                tensor.wait_ge(ac1.sem, 2)
                for tb in range(8):
                    if tb >= 2:
                        tensor.wait_ge(ac1.sem, 2 + tb - 1)
                    tensor.transpose(bank[2 + tb % 2][:, 0:8],
                                     lgfm[0:8, tb * 128:(tb + 1) * 128],
                                     ident[0:8, 0:8]).then_inc(pe1.sem, 1)
                    pe1.inc()
                tensor.wait_ge(po1.sem, 25)
                for tb in range(8):
                    if tb >= 2:
                        tensor.wait_ge(ac1.sem, 18 + tb - 1)
                    tensor.transpose(bank[4 + tb % 2][0:8, 0:128],
                                     wtetm[:, tb * 8:(tb + 1) * 8],
                                     ident[:]).then_inc(pe1.sem, 1)
                    pe1.inc()
                tensor.wait_ge(ac1.sem, 26)
                for j in range(16):
                    e, tc = divmod(j, 2)
                    if j >= 2:
                        tensor.wait_ge(ac1.sem, 26 + j - 1)
                    tensor.matmul(bank[6 + j % 2][:, :],
                                  oh8[:, e * 128:(e + 1) * 128],
                                  wtefm[0:8, tc * 512:(tc + 1) * 512],
                                  start=True, stop=True).then_inc(pe1.sem, 1)
                    pe1.inc()

            @b.scalar
            def _(scalar):
                scalar.wait_ge(pe1.sem, 1)
                scalar.copy(lgfm[0:8, 0:512], bank[0][0:8, :]).then_inc(ac1.sem, 1)
                ac1.inc()
                scalar.copy(lgfm[0:8, 512:1024], bank[1][0:8, :]).then_inc(ac1.sem, 1)
                ac1.inc()
                for tb in range(8):
                    scalar.wait_ge(pe1.sem, 2 + tb)
                    scalar.copy(lgtm[:, tb * 8:(tb + 1) * 8],
                                bank[2 + tb % 2][:, 0:8]).then_inc(ac1.sem, 1)
                    ac1.inc()
                scalar.wait_ge(dv1.sem, 8)
                for tb in range(8):
                    scalar.activation(etm[:, tb * 8:(tb + 1) * 8],
                                      lgtm[:, tb * 8:(tb + 1) * 8], AF.Exp,
                                      bias=nmx[:, tb:tb + 1]).then_inc(ac1.sem, 1)
                    ac1.inc()
                for tb in range(8):
                    scalar.wait_ge(pe1.sem, 9 + tb + 1)
                    scalar.copy(wtefm[0:8, tb * 128:(tb + 1) * 128],
                                bank[4 + tb % 2][0:8, 0:128]).then_inc(ac1.sem, 1)
                    ac1.inc()
                for j in range(16):
                    scalar.wait_ge(pe1.sem, 17 + j + 1)
                    e, tc = divmod(j, 2)
                    scalar.copy(wtebc[:, e * TL + tc * 512: e * TL + (tc + 1) * 512],
                                bank[6 + j % 2][:, :]).then_inc(ac1.sem, 1)
                    ac1.inc()

            @b.vector
            def _(vector):
                vector.wait_ge(ac1.sem, 10)
                for tb in range(8):
                    vector.tensor_reduce(nmx[:, tb:tb + 1], lgtm[:, tb * 8:(tb + 1) * 8],
                                         AX, ALU.max, negate=True).then_inc(dv1.sem, 1)
                    dv1.inc()
                vector.wait_ge(ac1.sem, 18)
                for tb in range(8):
                    sl8 = slice(tb * 8, (tb + 1) * 8)
                    vector.tensor_scalar(tmp64[:, sl8], etm[:, sl8], 1.0, -1e30,
                                         ALU.is_ge, ALU.mult).then_inc(dv1.sem, 1)
                    dv1.inc()
                for tb in range(8):
                    sl8 = slice(tb * 8, (tb + 1) * 8)
                    vector.wait_ge(po1.sem, tb + 1)
                    vector.tensor_reduce(m2c[:, tb:tb + 1], tmp64[:, sl8], AX,
                                         ALU.max).then_inc(dv1.sem, 1)
                    dv1.inc()
                vector.wait_ge(po1.sem, 17)
                vector.reciprocal(rdc[:], dnc[:]).then_inc(dv1.sem, 1)
                dv1.inc()
                for tb in range(8):
                    sl8 = slice(tb * 8, (tb + 1) * 8)
                    vector.tensor_tensor(wtetm[:, sl8], wtetm[:, sl8], etm[:, sl8],
                                         ALU.mult).then_inc(dv1.sem, 1)
                    dv1.inc()

        if upto < 2:
            with nc.Block() as b:
                @b.sync
                def _(sync):
                    dma(sync, ld, fdbg_o[0:128, 0:64], wtetm[:])
                    dma(sync, ld, fdbg_o[0:128, 64:128], lgtm[:])
                    dma(sync, ld, fdbg_o[0:128, 128:192], etm[:])
                    dma(sync, ld, fdbg_o[0:128, 192:200], m2c[:])
                    dma(sync, ld, fdbg_o[0:128, 200:208], rdc[:])
                    sync.wait_ge(ld.sem, ld.n)

                @b.gpsimd
                def _(gpsimd):
                    dma(gpsimd, ldg, fdbg_o[:, 64:64 + 8 * TL], wtebc[:, 0:8 * TL])
                    gpsimd.wait_ge(ldg.sem, ldg.n)
            return nc
        wdg = Ctr(nc, "wdg")
        wdd = SlotCtr(nc, "wdd", 2)
        pc2 = Ctr(nc, "pc2")
        evg = Ctr(nc, "evg")
        evd = Ctr(nc, "evd")
        evmap = {"g": evg, "d": evd}
        wg_order = []
        for g_ in groups:
            if g_["kind"] in ("gate", "up"):
                key = (g_["e"], 0 if g_["kind"] == "gate" else 1, g_["hi"])
                if key not in wg_order:
                    wg_order.append(key)
        down_tiles = [(g_["e"], g_["dblk"], g_["fk"]) for g_ in groups
                      if g_["kind"] == "down"]
        with nc.Block() as b:
            @b.gpsimd
            def _(gpsimd):
                for e in sorted({g_["e"] for g_ in groups}):
                    for i, (we, which, hi) in enumerate(wg_order):
                        if we != e:
                            continue
                        off, fw, nfc = HALVES[hi]
                        gidx = wg_order.index((we, which, hi))
                        if gidx > 0:
                            pe_, pw_, ph_ = wg_order[gidx - 1]
                            poff, pfw, pnfc = HALVES[ph_]
                            lastg = gl[("gate" if pw_ == 0 else "up", pe_,
                                        poff // 128 + pnfc - 1)]
                            gpsimd.wait_ge(pc2.sem, pc_idx[lastg])
                        warr = w_gate if which == 0 else w_up
                        dma(gpsimd, wdg, wgb[0:128, 0:16 * fw],
                            warr.ap()[e, :, off:off + fw].rearrange("(kc p) f -> p kc f", p=128))
                    for j, (de, dblk, fk) in enumerate(down_tiles):
                        if de != e:
                            continue
                        if j >= 2:
                            pe_, pd_, pf_ = down_tiles[j - 2]
                            pgi = gl[("down", pe_, pd_, pf_)]
                            gpsimd.wait_ge(pc2.sem, pc_idx[pgi])
                        wdd.dma(gpsimd, j, wdb[j % 2][:],
                                w_down.ap()[e, fk * 128:(fk + 1) * 128,
                                            dblk * 512:(dblk + 1) * 512])

            @b.tensor
            def _(tensor):
                nwg = 0
                for gi, g in enumerate(groups):
                    for ctr_name, v in bank_wait[gi].items():
                        tensor.wait_ge(evmap[ctr_name].sem, v)
                    if g["kind"] in ("gate", "up"):
                        if g["fc"] == 0:
                            nwg += 1
                            tensor.wait_ge(wdg.sem, 16 * nwg)
                        off, fw, fc = g["off"], g["fw"], g["fc"]
                        mm = None
                        for kc in range(16):
                            for tc in range(2):
                                mm = tensor.matmul(
                                    bank[(fc % 2) * 2 + tc][:, :],
                                    wgb[:, kc * fw + fc * 128: kc * fw + (fc + 1) * 128],
                                    xn2bf[:, kc * TL + tc * 512: kc * TL + (tc + 1) * 512],
                                    start=(kc == 0), stop=(kc == 15))
                        mm.then_inc(pc2.sem, 1)
                        pc2.inc()
                    else:
                        e, dblk, fk = g["e"], g["dblk"], g["fk"]
                        j = down_tiles.index((e, dblk, fk))
                        wdd.wait(tensor, j, j // 2 + 1)
                        upg = gl[("up", e, fk)]
                        tensor.wait_ge(evd.sem, done[upg][1])
                        mm = None
                        for dc4 in range(4):
                            for tc in range(2):
                                mm = tensor.matmul(
                                    bank[dc4 * 2 + tc][:, :],
                                    wdb[j % 2][:, dc4 * 128:(dc4 + 1) * 128],
                                    hbf[:, fk * TL + tc * 512: fk * TL + (tc + 1) * 512],
                                    start=(fk == 0), stop=(fk == 10))
                        mm.then_inc(pc2.sem, 1)
                        pc2.inc()

            @b.scalar
            def _(scalar):
                for gi, g in enumerate(groups):
                    if g["kind"] != "gate":
                        continue
                    e, fcg, fc = g["e"], g["fcg"], g["fc"]
                    scalar.wait_ge(pc2.sem, pc_idx[gi])
                    if e > 0:
                        pug = gl[("up", e - 1, fcg)]
                        scalar.wait_ge(evd.sem, done[pug][1])
                    for tc in range(2):
                        scalar.activation(
                            gsb[:, fcg * TL + tc * 512: fcg * TL + (tc + 1) * 512],
                            bank[(fc % 2) * 2 + tc][:, :],
                            AF.Silu).then_inc(evg.sem, 1)
                        evg.inc()

            @b.vector
            def _(vector):
                for gi, g in enumerate(groups):
                    if g["kind"] == "up":
                        e, fcg, fc = g["e"], g["fcg"], g["fc"]
                        vector.wait_ge(pc2.sem, pc_idx[gi])
                        gg = gl[("gate", e, fcg)]
                        vector.wait_ge(evg.sem, done[gg][1])
                        if e > 0:
                            lastdown = gl[("down", e - 1, 3, fcg)]
                            vector.wait_ge(pc2.sem, pc_idx[lastdown])
                        for tc in range(2):
                            sl = slice(fcg * TL + tc * 512, fcg * TL + (tc + 1) * 512)
                            vector.tensor_tensor(t5[:], bank[(fc % 2) * 2 + tc][:, :],
                                                 gsb[:, sl], ALU.mult)
                            vector.tensor_tensor(
                                hbf[:, sl], t5[:],
                                wtebc[:, e * TL + tc * 512: e * TL + (tc + 1) * 512],
                                ALU.mult).then_inc(evd.sem, 1)
                            evd.inc()
                    elif g["kind"] == "down" and g["fk"] == 10:
                        e, dblk = g["e"], g["dblk"]
                        vector.wait_ge(pc2.sem, pc_idx[gi])
                        for dc4 in range(4):
                            for tc in range(2):
                                dc = dblk * 4 + dc4
                                sl = slice(dc * TL + tc * 512, dc * TL + (tc + 1) * 512)
                                if e == 0:
                                    vector.tensor_scalar_mul(
                                        moe[:, sl], bank[dc4 * 2 + tc][:, :],
                                        1.0).then_inc(evd.sem, 1)
                                else:
                                    vector.tensor_tensor(
                                        moe[:, sl], moe[:, sl], bank[dc4 * 2 + tc][:, :],
                                        ALU.add).then_inc(evd.sem, 1)
                                evd.inc()

        if upto < 3:
            with nc.Block() as b:
                @b.sync
                def _(sync):
                    dma(sync, ld, fdbg_o[:, 64:64 + 8 * TL], moe[:, 0:8 * TL])
                    sync.wait_ge(ld.sem, ld.n)
            return nc
        ldyr = Ctr(nc, "ldyr")
        dvc = Ctr(nc, "dvc")
        acc = Ctr(nc, "acc")
        pec = Ctr(nc, "pec")
        soo = SlotCtr(nc, "soo", 2)
        silu_done = {}
        evac_done = {}
        with nc.Block() as b:
            @b.scalar
            def _(scalar):
                def emit_evacs(dcp):
                    for tb in range(8):
                        i = dcp * 8 + tb
                        scalar.wait_ge(pec.sem, i + 1)
                        if i >= 2:
                            soo.wait(scalar, i, i // 2)
                        scalar.copy(outst[:, (i % 2) * 128:(i % 2) * 128 + 128],
                                    bank[i % 2][:, 0:128]).then_inc(acc.sem, 1)
                        evac_done[i] = acc.inc()
                for dc in range(16):
                    if dc >= 1:
                        scalar.wait_ge(dvc.sem, 2 * dc)
                    scalar.activation(gst[:], moe[:, dc * TL:(dc + 1) * TL],
                                      AF.Silu).then_inc(acc.sem, 1)
                    silu_done[dc] = acc.inc()
                    if dc >= 1:
                        emit_evacs(dc - 1)
                emit_evacs(15)

            @b.sync
            def _(sync):
                for dc in range(16):
                    if dc >= 1:
                        sync.wait_ge(dvc.sem, 2 * dc)
                    dma(sync, ldyr, yb[:], y_fm[dc * 128:(dc + 1) * 128, :])
                    dma(sync, ldyr, rb[:], resid1[dc * 128:(dc + 1) * 128, :])
                    for tb in range(8):
                        i = dc * 8 + tb
                        sync.wait_ge(acc.sem, evac_done[i])
                        soo.dma(sync, i,
                                xout_o[tb * 128:(tb + 1) * 128, dc * 128:(dc + 1) * 128],
                                outst[:, (i % 2) * 128:(i % 2) * 128 + 128])
                for cc in soo.c:
                    sync.wait_ge(cc.sem, cc.n)

            @b.vector
            def _(vector):
                for dc in range(16):
                    vector.wait_ge(acc.sem, silu_done[dc])
                    vector.wait_ge(ldyr.sem, 32 * (dc + 1))
                    if dc >= 1:
                        vector.wait_ge(pec.sem, 8 * dc)
                    vector.tensor_tensor(xos[:], gst[:], yb[:], ALU.mult)
                    vector.tensor_tensor(xos[:], xos[:], rb[:],
                                         ALU.add).then_inc(dvc.sem, 2)
                    dvc.inc(2)

            @b.tensor
            def _(tensor):
                for dc in range(16):
                    tensor.wait_ge(dvc.sem, 2 * (dc + 1))
                    for tb in range(8):
                        i = dc * 8 + tb
                        if i >= 2:
                            tensor.wait_ge(acc.sem, evac_done[i - 2])
                        tensor.transpose(bank[i % 2][:, 0:128],
                                         xos[:, tb * 128:(tb + 1) * 128],
                                         ident[:]).then_inc(pec.sem, 1)
                        pec.inc()

    return nc


def shard_inputs_F(resC, resD, resE, inputs, cores=range(8)):
    y_g = np.empty((2, 2048, 4096), np.float32)
    for c in range(8):
        r0 = c * 256
        y_g[:, r0:r0 + 256, :] = resE[c]["y_sl"]
    gwT = np.asarray(inputs["L1_ffn_gate_w"], np.float32).T
    gw2 = np.ascontiguousarray(gwT.reshape(16, 128, 8).transpose(1, 0, 2).reshape(128, 128))
    C = host_constants()
    wg = np.ascontiguousarray(np.asarray(inputs["L1_ffn_w_gate"], np.float32))
    wu = np.ascontiguousarray(np.asarray(inputs["L1_ffn_w_up"], np.float32))
    wd = np.ascontiguousarray(np.asarray(inputs["L1_ffn_w_down"], np.float32))
    in_maps = []
    for c in cores:
        bb, sl = c // 4, c % 4
        in_maps.append(dict(
            resid1_fm=resC[c]["resid1_fm"],
            xn2_bf=resD[c]["xn2_bf"],
            y_fm=np.ascontiguousarray(y_g[bb, :, sl * TL:(sl + 1) * TL]),
            gw2=gw2, w_gate=wg, w_up=wu, w_down=wd,
            ident=C["ident"], oh8=C["oh8_bf"],
        ))
    return in_maps


def run_F(resC, resD, resE, inputs, cores=range(8), upto=3):
    nc = build_F(upto)
    res = run_bass_kernel_spmd(nc, shard_inputs_F(resC, resD, resE, inputs, cores),
                               core_ids=list(cores))
    return res.results


def kernel(**inputs):
    cores = list(range(8))
    resA = run_A(inputs, cores)
    resB = run_B(resA, inputs, cores)
    resC = run_C(resA, resB, inputs, cores)
    resD = run_D(resC, inputs, cores)
    resE = run_E(resD, inputs, cores)
    resF = run_F(resC, resD, resE, inputs, cores)
    out = np.concatenate([resF[c]["xout"] for c in range(8)], axis=0)
    return np.ascontiguousarray(out.reshape(2, 4096, 2048))
